# version 01
# baseline (speedup 1.0000x reference)
import numpy as np
import ml_dtypes
from contextlib import ExitStack
import concourse.bass as bass
import concourse.mybir as mybir
from concourse.bass_utils import run_bass_kernel_spmd

F32 = mybir.dt.float32
BF16 = mybir.dt.bfloat16
AF = mybir.ActivationFunctionType
ALU = mybir.AluOpType
AX = mybir.AxisListType

MAGIC = 12582912.0
SC_ATT = 0.08838834764831845
NEG_HALF = -0.5
EPS = 1e-5
T, D, DI = 2048, 2048, 8192
NT, ND = 16, 16
NKV = 8
NOB, NOC = 16, 64

DEBUG = False
UPTO = 99
ENGS = ("sync", "tensor", "vector", "scalar", "gpsimd")


class B:
    def __init__(self):
        self.ops = {k: [] for k in ENGS}
        self.cnt = {}
        self.semh = {}
        self.lastw = {e: {} for e in ENGS}

    def emit(self, eng, fn):
        self.ops[eng].append(fn)

    def wait(self, eng, evt):
        if evt is None:
            return
        sem, val, peng = evt
        if peng == eng or val <= 0:
            return
        if self.lastw[eng].get(sem, 0) >= val:
            return
        self.lastw[eng][sem] = val

        def f(e, sem=sem, val=val):
            e.wait_ge(self.semh[sem], val)
        self.emit(eng, f)

    def waits(self, eng, evts):
        for ev in evts:
            self.wait(eng, ev)

    def bump(self, sem, k):
        self.cnt[sem] = self.cnt.get(sem, 0) + k
        return self.cnt[sem]

    DMA_POOL = {"sync": ["dl0", "dl1", "dl2", "dl3"],
                "scalar": ["ds0", "ds1"],
                "gpsimd": ["dg0", "dg1", "dg2", "dg3"]}

    def dma(self, eng, dst, src):
        pool = self.DMA_POOL[eng]
        k = self.cnt.get("_rr_" + eng, 0)
        self.cnt["_rr_" + eng] = k + 1
        sem = pool[k % len(pool)]
        before = self.cnt.get(sem, 0)
        self.wait(eng, (sem, before, None))
        val = self.bump(sem, 16)

        def f(e, dst=dst, src=src, sem=sem):
            e.dma_start(out=dst, in_=src).then_inc(self.semh[sem], 16)
        self.emit(eng, f)
        return (sem, val, None)

    def do(self, eng, fn, mark=None):
        if mark is None:
            self.emit(eng, lambda e, fn=fn: fn(e))
            return None
        val = self.bump(mark, 1)

        def f(e, fn=fn, mark=mark):
            fn(e).then_inc(self.semh[mark], 1)
        self.emit(eng, f)
        return (mark, val, eng)

    def barrier(self):
        for eng, pool in self.DMA_POOL.items():
            for sem in pool:
                self.wait(eng, (sem, self.cnt.get(sem, 0), None))
        tgt = self.bump("bar", len(ENGS))
        for eng in ENGS:
            def f(e):
                try:
                    e.drain()
                except Exception:
                    pass
                e.nop().then_inc(self.semh["bar"], 1)
            self.emit(eng, f)
        for eng in ENGS:
            def f(e, tgt=tgt):
                e.wait_ge(self.semh["bar"], tgt)
            self.emit(eng, f)
            self.lastw[eng]["bar"] = tgt


def build_nc(debug=False, upto=99):
    nc = bass.Bass("TRN2", target_bir_lowering=False, debug=False)
    b = B()

    def inp(name, shape, dtype=F32):
        return nc.dram_tensor(name, shape, dtype, kind="ExternalInput").ap()

    x_h = inp("x", [T, D])
    wkvq_h = inp("wkvq", [D, 6144])
    wout_h = inp("woutT", [4096, D])
    w1_h = inp("w1T", [D, DI])
    w2_h = inp("w2T", [DI, D])
    b1_h = inp("b1", [1, DI])
    g2_h = inp("lng", [1, DI])
    bb2_h = inp("lnb", [1, DI])
    b2_h = inp("b2", [1, D])
    cos_h = inp("cosT", [128, T])
    sin_h = inp("sinT", [128, T])
    tri_h = inp("trimask", [128, 512])
    id_h = inp("ident", [128, 128])
    y_h = nc.dram_tensor("y", [T, D], F32, kind="ExternalOutput").ap()

    skind = "ExternalOutput" if debug else "Internal"

    def scr(name, shape, dtype):
        return nc.dram_tensor(name, shape, dtype, kind=skind).ap()

    xq1_sc = scr("xq1_sc", [NT, ND, 128, 128], BF16)
    q_sc = scr("q_sc", [NKV, 128, NT, 4, 128], BF16)
    at_sc = scr("at_sc", [NKV, 128, NT, 512], BF16)
    s_sc = scr("s_sc", [NT, 128, D], BF16)
    h_sc = scr("h_sc", [NT, 128, DI], F32)
    xq2_sc = scr("xq2_sc", [NOC, 128, D], BF16)
    w1t_sc = scr("w1t_sc", [NOB, ND, 128, 512], BF16)
    w2t_sc = scr("w2t_sc", [ND, NOC, 128, 128], BF16)
    dbg = {}
    if debug:
        for nm, sh, dt in (("d_xnfm", [128, ND * T], BF16),
                           ("d_kfm", [128, NKV * T], BF16),
                           ("d_vkm", [128, NKV * T], BF16),
                           ("d_stats", [128, 64], F32),
                           ("d_wsc", [128, 8], F32)):
            dbg[nm] = nc.dram_tensor(nm, sh, dt, kind="ExternalOutput").ap()

    ctx = ExitStack()
    e = ctx.enter_context
    for s in ("dl0", "dl1", "dl2", "dl3", "ds0", "ds1", "dg0", "dg1",
              "dg2", "dg3", "pe", "dv", "ac", "bar"):
        b.semh[s] = e(nc.semaphore(s))

    cpool = e(nc.sbuf_tensor("cpool", [128, 4], F32))
    nc.const_aps.register(F32, SC_ATT, cpool[:, 0:1])
    nc.const_aps.register(F32, NEG_HALF, cpool[:, 1:2])
    onec = e(nc.sbuf_tensor("onec", [1, 128], F32))
    onecol = e(nc.sbuf_tensor("onecol", [128, 1], F32))
    one11 = e(nc.sbuf_tensor("one11", [1, 1], F32))
    ones_bf = e(nc.sbuf_tensor("ones_bf", [128, 128], BF16))
    identb = e(nc.sbuf_tensor("identb", [128, 128], BF16))
    identf = e(nc.sbuf_tensor("identf", [128, 128], F32))
    tri = e(nc.sbuf_tensor("tri", [128, 512], F32))
    m1c = e(nc.sbuf_tensor("m1c", [128, NT], F32))
    c1c = e(nc.sbuf_tensor("c1c", [128, NT], F32))
    c2c = e(nc.sbuf_tensor("c2c", [128, NT], F32))
    wsc = e(nc.sbuf_tensor("wsc", [128, 8], F32))
    b2cols = e(nc.sbuf_tensor("b2cols", [128, ND], F32))

    b.dma("sync", tri[:], tri_h[:, :])
    ev_idf = b.dma("sync", identf[:], id_h[:, :])

    def init_consts(v):
        v.memset(onec[:], 1.0)
        v.memset(onecol[:], 1.0)
        v.memset(one11[:], 1.0)
        v.memset(ones_bf[:], 1.0)
        v.memset(cpool[:, 0:1], SC_ATT)
        return v.memset(cpool[:, 1:2], NEG_HALF)
    ev_cst = b.do("vector", init_consts, mark="dv")
    b.wait("vector", ev_idf)
    ev_idb = b.do("vector", lambda v: v.tensor_copy(identb[:], identf[:]),
                  mark="dv")

    xn_ctx = ExitStack()
    xn_fm = xn_ctx.enter_context(nc.sbuf_tensor("xn_fm", [128, ND * T], BF16))
    rope_ctx = ExitStack()
    er = rope_ctx.enter_context
    cosT = er(nc.sbuf_tensor("cosT_sb", [128, T], F32))
    ssT = er(nc.sbuf_tensor("ssT_sb", [128, T], F32))
    b.dma("sync", cosT[:], cos_h[:, :])
    b.dma("sync", ssT[:], sin_h[:, :])

    if upto >= 1:
        pa = ExitStack()
        ea = pa.enter_context
        xld = [ea(nc.sbuf_tensor(f"xld{j}", [128, D], F32)) for j in range(2)]
        xn32 = ea(nc.sbuf_tensor("xn32", [128, D], F32))
        tq32 = ea(nc.sbuf_tensor("tq32", [128, D], F32))
        xnbf = ea(nc.sbuf_tensor("xnbf", [128, D], BF16))
        xqbf = ea(nc.sbuf_tensor("xqbf", [128, D], BF16))
        xqst = ea(nc.sbuf_tensor("xqst", [128, D], BF16))
        trashA = ea(nc.sbuf_tensor("trashA", [128, D], BF16))
        stc = ea(nc.sbuf_tensor("stc", [128, 20], F32))
        pA_ps = ExitStack()
        pT = pA_ps.enter_context(nc.psum_tensor("pTA", [128, 512], BF16))
        pT2 = pA_ps.enter_context(nc.psum_tensor("pTA2", [128, 512], BF16))

        (SU, SQ, MU, EX2, MU2, VAR, VEPS, LNV, INV, I2, MX2, M2E, LNM, RST,
         MX, QS, RM) = range(17)

        xld_free = [None, None]
        st_free = None
        for i in range(NT):
            buf = xld[i % 2]
            b.wait("sync", xld_free[i % 2])
            ev_x = b.dma("sync", buf[:], x_h[i * 128:(i + 1) * 128, :])
            b.wait("vector", ev_x)
            ev_ms = b.do("vector", lambda v, bu=buf: (
                v.drain(),
                v.reduce_sum(stc[:, SU:SU + 1], bu[:], AX.X),
                v.memset(stc[:, SQ:SQ + 1], 0.0))[-1], mark="dv")
            b.waits("scalar", [ev_x, ev_ms])
            ev_sq = b.do("scalar", lambda s, bu=buf: s.activation(
                trashA[:], bu[:], AF.Square, accum_out=stc[:, SQ:SQ + 1]),
                mark="ac")
            b.wait("vector", ev_sq)

            def stats1(v):
                v.tensor_scalar_mul(stc[:, MU:MU + 1], stc[:, SU:SU + 1],
                                    1.0 / D)
                v.tensor_scalar_mul(stc[:, EX2:EX2 + 1], stc[:, SQ:SQ + 1],
                                    1.0 / D)
                v.drain()
                v.tensor_mul(stc[:, MU2:MU2 + 1], stc[:, MU:MU + 1],
                             stc[:, MU:MU + 1])
                v.drain()
                v.tensor_sub(stc[:, VAR:VAR + 1], stc[:, EX2:EX2 + 1],
                             stc[:, MU2:MU2 + 1])
                v.drain()
                return v.tensor_scalar_add(stc[:, VEPS:VEPS + 1],
                                           stc[:, VAR:VAR + 1], EPS)
            ev_v1 = b.do("vector", stats1, mark="dv")
            b.wait("scalar", ev_v1)

            def act_inv(s):
                s.activation(stc[:, LNV:LNV + 1], stc[:, VEPS:VEPS + 1], AF.Ln)
                s.drain()
                return s.activation(stc[:, INV:INV + 1], stc[:, LNV:LNV + 1],
                                    AF.Exp, scale=NEG_HALF)
            ev_inv = b.do("scalar", act_inv, mark="ac")
            b.wait("vector", ev_inv)

            def xnf(v, bu=buf):
                v.drain()
                v.tensor_scalar(xn32[:], bu[:], stc[:, MU:MU + 1],
                                stc[:, INV:INV + 1], ALU.subtract, ALU.mult)
                v.tensor_mul(stc[:, I2:I2 + 1], stc[:, INV:INV + 1],
                             stc[:, INV:INV + 1])
                v.drain()
                v.tensor_copy(xnbf[:], xn32[:])
                v.tensor_mul(stc[:, MX2:MX2 + 1], stc[:, VAR:VAR + 1],
                             stc[:, I2:I2 + 1])
                v.reduce_max(stc[:, MX:MX + 1], xn32[:], AX.X,
                             apply_absolute_value=True)
                v.drain()
                return v.tensor_scalar_add(stc[:, M2E:M2E + 1],
                                           stc[:, MX2:MX2 + 1], EPS)
            ev_v2 = b.do("vector", xnf, mark="dv")
            xld_free[i % 2] = ev_v2
            b.wait("scalar", ev_v2)

            def act_rst(s):
                s.activation(stc[:, LNM:LNM + 1], stc[:, M2E:M2E + 1], AF.Ln)
                s.drain()
                return s.activation(stc[:, RST:RST + 1], stc[:, LNM:LNM + 1],
                                    AF.Exp, scale=NEG_HALF)
            ev_rst = b.do("scalar", act_rst, mark="ac")
            b.wait("vector", ev_rst)

            def quant(v, i=i):
                v.tensor_mul(m1c[:, i:i + 1], stc[:, RST:RST + 1],
                             stc[:, MX:MX + 1])
                v.drain()
                v.tensor_scalar_max(m1c[:, i:i + 1], m1c[:, i:i + 1], EPS)
                v.drain()
                v.reciprocal(stc[:, RM:RM + 1], m1c[:, i:i + 1])
                v.drain()
                v.tensor_mul(stc[:, QS:QS + 1], stc[:, RST:RST + 1],
                             stc[:, RM:RM + 1])
                v.drain()
                v.tensor_scalar_mul(stc[:, QS:QS + 1], stc[:, QS:QS + 1],
                                    127.0)
                v.drain()
                v.tensor_scalar_mul(tq32[:], xn32[:], stc[:, QS:QS + 1])
                v.drain()
                return v.tensor_scalar(xqbf[:], tq32[:], MAGIC, MAGIC,
                                       ALU.add, ALU.subtract)
            ev_q = b.do("vector", quant, mark="dv")

            b.waits("tensor", [ev_v2, ev_idb])
            for gdc in range(4):
                def tr4(t, gdc=gdc):
                    ins = None
                    for j in range(4):
                        dc = gdc * 4 + j
                        ins = t.transpose(pT[:, j * 128:(j + 1) * 128],
                                          xnbf[:, dc * 128:(dc + 1) * 128],
                                          identb[:])
                    return ins
                ev_t = b.do("tensor", tr4, mark="pe")
                b.wait("vector", ev_t)

                def evac(v, gdc=gdc, i=i):
                    dst = xn_fm[:].rearrange("p (dc t) -> p dc t", dc=ND)[
                        :, gdc * 4:(gdc + 1) * 4, i * 128:(i + 1) * 128]
                    return v.tensor_copy(
                        dst, pT[:].rearrange("p (j t) -> p j t", j=4))
                ev_ev = b.do("vector", evac, mark="dv")
                b.wait("tensor", ev_ev)
            b.waits("tensor", [ev_q, st_free])
            last_evq = None
            for gdc in range(4):
                def trq(t, gdc=gdc):
                    ins = None
                    for j in range(4):
                        dc = gdc * 4 + j
                        ins = t.transpose(pT2[:, j * 128:(j + 1) * 128],
                                          xqbf[:, dc * 128:(dc + 1) * 128],
                                          identb[:])
                    return ins
                ev_t = b.do("tensor", trq, mark="pe")
                b.wait("vector", ev_t)

                def evacq(v, gdc=gdc):
                    dst = xqst[:].rearrange("p (dc t) -> p dc t", dc=ND)[
                        :, gdc * 4:(gdc + 1) * 4, :]
                    return v.tensor_copy(
                        dst, pT2[:].rearrange("p (j t) -> p j t", j=4))
                last_evq = b.do("vector", evacq, mark="dv")
                b.wait("tensor", last_evq)
            b.wait("gpsimd", last_evq)
            st_free = b.dma("gpsimd",
                            xq1_sc[i].rearrange("dc d t -> d dc t"),
                            xqst[:].rearrange("p (dc t) -> p dc t", dc=ND))
        if debug:
            b.wait("gpsimd", ("dv", b.cnt.get("dv", 0), None))
            b.dma("gpsimd", dbg["d_xnfm"][:, :], xn_fm[:])
            b.dma("gpsimd", dbg["d_stats"][:, 0:NT], m1c[:])
        b.barrier()
        pa.close()
        pA_ps.close()

    if upto >= 2:
        pw = ExitStack()
        ew = pw.enter_context
        wch = [ew(nc.sbuf_tensor(f"wch{j}", [128, DI], F32)) for j in range(2)]
        acc = ew(nc.sbuf_tensor("wacc", [128, 4], F32))
        ternb = [ew(nc.sbuf_tensor(f"tern{j}", [128, DI], BF16))
                 for j in range(2)]
        negb = ew(nc.sbuf_tensor("negb", [128, DI], BF16))
        srow = ew(nc.sbuf_tensor("srow", [1, 8], F32))
        pw_ps = ExitStack()
        pW = pw_ps.enter_context(nc.psum_tensor("pW", [1, 2], F32))
        pC = pw_ps.enter_context(nc.psum_tensor("pC", [128, 8], F32))

        ch_free = [None, None]
        tern_free = [None, None]
        for wi, (w_h, nch, F, nelem) in enumerate(
                ((w1_h, 16, DI, D * DI), (w2_h, 64, D, DI * D))):
            ev_z = b.do("vector", lambda v, wi=wi: v.memset(
                acc[:, wi:wi + 1], 0.0), mark="dv")
            for c in range(nch):
                buf = wch[c % 2]
                b.wait("sync", ch_free[c % 2])
                ev_w = b.dma("sync", buf[:, 0:F],
                             w_h[c * 128:(c + 1) * 128, :])
                b.waits("vector", [ev_w, ev_z])

                def statw(v, buf=buf, F=F, wi=wi):
                    v.drain()
                    v.tensor_reduce(acc[:, 2:3], buf[:, 0:F], AX.X, ALU.add,
                                    apply_absolute_value=True)
                    v.drain()
                    return v.tensor_add(acc[:, wi:wi + 1], acc[:, wi:wi + 1],
                                        acc[:, 2:3])
                ch_free[c % 2] = b.do("vector", statw, mark="dv")
            b.waits("tensor", [ch_free[(nch - 1) % 2], ev_cst])
            ev_tot = b.do("tensor", lambda t, wi=wi: t.matmul(
                pW[0:1, wi:wi + 1], acc[:, wi:wi + 1], onecol[:, 0:1],
                start=True, stop=True), mark="pe")
            b.wait("vector", ev_tot)

            def scals(v, wi=wi, nelem=nelem):
                v.tensor_scalar_mul(srow[0:1, 4 * wi:4 * wi + 1],
                                    pW[0:1, wi:wi + 1], 1.0 / nelem)
                v.drain()
                v.tensor_scalar_max(srow[0:1, 4 * wi + 3:4 * wi + 4],
                                    srow[0:1, 4 * wi:4 * wi + 1], EPS)
                v.drain()
                v.tensor_scalar_mul(srow[0:1, 4 * wi + 1:4 * wi + 2],
                                    srow[0:1, 4 * wi + 3:4 * wi + 4], 0.5)
                return v.tensor_scalar_mul(
                    srow[0:1, 4 * wi + 2:4 * wi + 3],
                    srow[0:1, 4 * wi + 3:4 * wi + 4], -0.5)
            ev_sc = b.do("vector", scals, mark="dv")
            b.wait("tensor", ev_sc)
            ev_bc = b.do("tensor", lambda t, wi=wi: t.matmul(
                pC[:, 3 * wi:3 * wi + 3],
                onec[0:1, :],
                srow[0:1, 4 * wi + 1:4 * wi + 4],
                start=True, stop=True), mark="pe")
            b.wait("vector", ev_bc)
            ev_wsc = b.do("vector", lambda v, wi=wi: (
                v.tensor_copy(wsc[:, 3 * wi:3 * wi + 3],
                              pC[:, 3 * wi:3 * wi + 3]),
                v.drain())[0], mark="dv")
            for c in range(nch):
                buf = wch[c % 2]
                b.wait("sync", ch_free[c % 2])
                ev_w = b.dma("sync", buf[:, 0:F],
                             w_h[c * 128:(c + 1) * 128, :])
                tb = ternb[c % 2]
                b.waits("vector", [ev_w, ev_wsc, tern_free[c % 2]])

                def ternf(v, buf=buf, tb=tb, F=F, wi=wi):
                    v.drain()
                    v.tensor_single_scalar(negb[:, 0:F], buf[:, 0:F],
                                           wsc[:, 3 * wi + 1:3 * wi + 2],
                                           ALU.is_lt)
                    v.drain()
                    return v.scalar_tensor_tensor(
                        tb[:, 0:F], buf[:, 0:F],
                        wsc[:, 3 * wi:3 * wi + 1], negb[:, 0:F],
                        ALU.is_gt, ALU.subtract)
                ev_t = b.do("vector", ternf, mark="dv")
                ch_free[c % 2] = ev_t
                b.wait("gpsimd", ev_t)
                if wi == 0:
                    dst = w1t_sc[:, c, :, :].rearrange("ob d o -> d ob o")
                    src = tb[:, 0:F].rearrange("d (ob o) -> d ob o", ob=NOB)
                else:
                    dst = w2t_sc[:, c, :, :].rearrange("dpc o dp -> o dpc dp")
                    src = tb[:, 0:F].rearrange("o (dpc dp) -> o dpc dp",
                                               dpc=ND)
                tern_free[c % 2] = b.dma("gpsimd", dst, src)
            if wi == 0:
                ev_c1 = b.do("vector", lambda v: (
                    v.drain(),
                    v.tensor_scalar(c1c[:], m1c[:], wsc[:, 2:3], 1.0 / 127.0,
                                    ALU.mult, ALU.mult))[-1], mark="dv")
        if debug:
            b.wait("gpsimd", ("dv", b.cnt.get("dv", 0), None))
            b.dma("gpsimd", dbg["d_wsc"][:, 0:6], wsc[:, 0:6])
        b.barrier()
        pw.close()
        pw_ps.close()

    kv_ctx = ExitStack()
    k_fm = kv_ctx.enter_context(
        nc.sbuf_tensor("k_fm", [128, NKV * T], BF16, side="right"))
    v_km = kv_ctx.enter_context(
        nc.sbuf_tensor("v_km", [128, NKV * T], BF16, side="right"))
    if upto >= 3:
        p1 = ExitStack()
        e1 = p1.enter_context
        wres = e1(nc.sbuf_tensor("wres1", [128, 16 * 1024], BF16))
        wstg = [e1(nc.sbuf_tensor(f"wstg{j}", [128, 1024], F32))
                for j in range(2)]
        a32 = e1(nc.sbuf_tensor("a32", [128, 1024], F32))
        b32 = e1(nc.sbuf_tensor("b32", [128, 1024], F32))
        vstg = e1(nc.sbuf_tensor("vstg", [128, 1024], BF16))
        p1_ps = ExitStack()
        pA = p1_ps.enter_context(nc.psum_tensor("pA1", [128, 1024], F32))
        pB = p1_ps.enter_context(nc.psum_tensor("pB1", [128, 1024], F32))
        pTv = p1_ps.enter_context(nc.psum_tensor("pTv", [128, 512], BF16))

        last_pe_half = None
        stg_free = [None, None]
        prev_evac = [None, None]
        for half in range(2):
            cast_evs = []
            for dc in range(ND):
                sb = wstg[dc % 2]
                b.wait("sync", stg_free[dc % 2])
                ev_w = b.dma("sync", sb[:],
                             wkvq_h[dc * 128:(dc + 1) * 128,
                                    half * 1024:(half + 1) * 1024])
                b.waits("scalar", [ev_w, last_pe_half])
                ev_c = b.do("scalar", lambda s, sb=sb, dc=dc: s.copy(
                    wres[:, dc * 1024:(dc + 1) * 1024], sb[:]), mark="ac")
                stg_free[dc % 2] = ev_c
                cast_evs.append(ev_c)
            b.waits("tensor", cast_evs)
            unit = 0
            for oc in range(8):
                for tb in range(2):
                    ps = pA if unit % 2 == 0 else pB
                    b.wait("tensor", prev_evac[unit % 2])

                    def mmkv(t, oc=oc, tb=tb, ps=ps):
                        ins = None
                        for dc in range(ND):
                            for nb in range(2):
                                ins = t.matmul(
                                    ps[:, nb * 512:(nb + 1) * 512],
                                    wres[:, dc * 1024 + oc * 128:
                                         dc * 1024 + (oc + 1) * 128],
                                    xn_fm[:, dc * T + tb * 1024 + nb * 512:
                                          dc * T + tb * 1024 +
                                          (nb + 1) * 512],
                                    start=(dc == 0), stop=(dc == ND - 1))
                        return ins
                    ev_mm = b.do("tensor", mmkv, mark="pe")
                    b.wait("vector", ev_mm)
                    if half == 0:
                        ev_cp = b.do("vector", lambda v, ps=ps:
                                     v.tensor_copy(vstg[:], ps[:]),
                                     mark="dv")
                        prev_evac[unit % 2] = ev_cp
                        b.wait("tensor", ev_cp)
                        for grp in range(2):
                            def trv(t, grp=grp):
                                ins = None
                                for j in range(4):
                                    ins = t.transpose(
                                        pTv[:, j * 128:(j + 1) * 128],
                                        vstg[:, (grp * 4 + j) * 128:
                                             (grp * 4 + j + 1) * 128],
                                        identb[:])
                                return ins
                            ev_t = b.do("tensor", trv, mark="pe")
                            b.wait("vector", ev_t)

                            def evv(v, oc=oc, tb=tb, grp=grp):
                                kt0 = oc * 16 + tb * 8 + grp * 4
                                return v.tensor_copy(
                                    v_km[:, kt0 * 128:(kt0 + 4) * 128],
                                    pTv[:])
                            ev_e = b.do("vector", evv, mark="dv")
                            b.wait("tensor", ev_e)
                    else:
                        def ropek(v, ps=ps, oc=oc, tb=tb):
                            sl = slice(tb * 1024, (tb + 1) * 1024)
                            v.drain()
                            v.tensor_mul(a32[:], ps[:], cosT[:, sl])
                            v.tensor_mul(b32[0:64, :], ps[64:128, :],
                                         ssT[0:64, sl])
                            v.tensor_mul(b32[64:128, :], ps[0:64, :],
                                         ssT[64:128, sl])
                            v.drain()
                            return v.tensor_add(
                                k_fm[:, oc * T + tb * 1024:
                                     oc * T + (tb + 1) * 1024],
                                a32[:], b32[:])
                        ev_r = b.do("vector", ropek, mark="dv")
                        prev_evac[unit % 2] = ev_r
                    unit += 1
            last_pe_half = ("pe", b.cnt.get("pe", 0), "tensor")
        if debug:
            b.wait("gpsimd", ("dv", b.cnt.get("dv", 0), None))
            b.dma("gpsimd", dbg["d_kfm"][:, :], k_fm[:])
            b.dma("gpsimd", dbg["d_vkm"][:, :], v_km[:])
        b.barrier()
        p1.close()
        p1_ps.close()

    if upto >= 4:
        p2 = ExitStack()
        e2 = p2.enter_context
        wres2 = e2(nc.sbuf_tensor("wres2", [128, 16 * 1024], BF16))
        wstg2 = e2(nc.sbuf_tensor("wstg2", [128, 1024], F32))
        a32q = e2(nc.sbuf_tensor("a32q", [128, 1024], F32))
        b32q = e2(nc.sbuf_tensor("b32q", [128, 1024], F32))
        qstg = [e2(nc.sbuf_tensor(f"qstg{j}", [128, 1024], BF16))
                for j in range(2)]
        p2_ps = ExitStack()
        pA2 = p2_ps.enter_context(nc.psum_tensor("pA2", [128, 1024], F32))
        pB2 = p2_ps.enter_context(nc.psum_tensor("pB2", [128, 1024], F32))

        last_pe_qr = None
        stg_free = None
        prev_rope = [None, None]
        st_free = [None, None]
        for qr in range(4):
            cast_evs = []
            for dc in range(ND):
                b.wait("sync", stg_free)
                ev_w = b.dma("sync", wstg2[:],
                             wkvq_h[dc * 128:(dc + 1) * 128,
                                    2048 + qr * 1024:2048 + (qr + 1) * 1024])
                b.waits("scalar", [ev_w, last_pe_qr])
                ev_c = b.do("scalar", lambda s, dc=dc: s.copy(
                    wres2[:, dc * 1024:(dc + 1) * 1024], wstg2[:]),
                    mark="ac")
                stg_free = ev_c
                cast_evs.append(ev_c)
            b.waits("tensor", cast_evs)
            unit = 0
            for oc in range(8):
                h = qr * 8 + oc
                for tb in range(2):
                    ps = pA2 if unit % 2 == 0 else pB2
                    b.wait("tensor", prev_rope[unit % 2])

                    def mmq(t, oc=oc, tb=tb, ps=ps):
                        ins = None
                        for dc in range(ND):
                            for nb in range(2):
                                ins = t.matmul(
                                    ps[:, nb * 512:(nb + 1) * 512],
                                    wres2[:, dc * 1024 + oc * 128:
                                          dc * 1024 + (oc + 1) * 128],
                                    xn_fm[:, dc * T + tb * 1024 + nb * 512:
                                          dc * T + tb * 1024 +
                                          (nb + 1) * 512],
                                    start=(dc == 0), stop=(dc == ND - 1))
                        return ins
                    ev_mm = b.do("tensor", mmq, mark="pe")
                    qs_buf = qstg[unit % 2]
                    b.waits("vector", [ev_mm, st_free[unit % 2]])

                    def ropeq(v, ps=ps, tb=tb, qs_buf=qs_buf):
                        sl = slice(tb * 1024, (tb + 1) * 1024)
                        v.drain()
                        v.tensor_mul(a32q[:], ps[:], cosT[:, sl])
                        v.tensor_mul(b32q[0:64, :], ps[64:128, :],
                                     ssT[0:64, sl])
                        v.tensor_mul(b32q[64:128, :], ps[0:64, :],
                                     ssT[64:128, sl])
                        v.drain()
                        return v.tensor_add(qs_buf[:], a32q[:], b32q[:])
                    ev_r = b.do("vector", ropeq, mark="dv")
                    prev_rope[unit % 2] = ev_r
                    b.wait("gpsimd", ev_r)
                    st_free[unit % 2] = b.dma(
                        "gpsimd",
                        q_sc[h // 4, :, tb * 8:(tb + 1) * 8, h % 4, :],
                        qs_buf[:].rearrange("p (qt t) -> p qt t", qt=8))
                    unit += 1
            last_pe_qr = ("pe", b.cnt.get("pe", 0), "tensor")
        b.barrier()
        p2.close()
        p2_ps.close()
    rope_ctx.close()
    xn_ctx.close()

    if upto >= 5:
        p3 = ExitStack()
        e3 = p3.enter_context
        qbuf = [e3(nc.sbuf_tensor(f"qbuf{j}", [128, 4096], BF16))
                for j in range(2)]
        Pp = [e3(nc.sbuf_tensor(f"Pp{j}", [128, 512], BF16))
              for j in range(2)]
        zrec = e3(nc.sbuf_tensor("zrec", [128, 512], F32))
        atile = [e3(nc.sbuf_tensor(f"atile{j}", [128, 512], BF16))
                 for j in range(2)]
        p3_ps = ExitStack()
        pS = [p3_ps.enter_context(nc.psum_tensor(f"pS{j}", [128, 512], F32))
              for j in range(2)]
        pAV = p3_ps.enter_context(nc.psum_tensor("pAV", [128, 512], F32))
        pZ = p3_ps.enter_context(nc.psum_tensor("pZ", [128, 512], F32))

        q_free = [None, None]
        at_free = [None, None]
        ev_evac = None
        ai = 0
        for qt in range(NT):
            qb = qbuf[qt % 2]
            b.wait("sync", q_free[qt % 2])
            ev_q = b.dma("sync", qb[:].rearrange("p (kvh f) -> p kvh f",
                                                 kvh=NKV),
                         q_sc[:, :, qt, :, :].rearrange(
                             "kvh hd g t -> hd kvh (g t)"))
            last_smm = None
            exp_evs = {}
            pe_z = {}
            g = 0
            for kvh in range(NKV):

                def emit_smm(kt, g, kvh=kvh, qt=qt, qb=qb):
                    par = g % 2
                    b.waits("tensor", [ev_q, exp_evs.get(g - 2), ev_evac])

                    def smm(t, kvh=kvh, kt=kt, par=par, qb=qb):
                        return t.matmul(
                            pS[par][:],
                            k_fm[:, kvh * T + kt * 128:kvh * T +
                                 (kt + 1) * 128],
                            qb[:, kvh * 512:(kvh + 1) * 512],
                            start=True, stop=True)
                    ev_s = b.do("tensor", smm, mark="pe")
                    src_ev = ev_s
                    if kt == qt:
                        b.wait("vector", ev_s)
                        ev_m = b.do("vector", lambda v, par=par:
                                    v.tensor_tensor(pS[par][:], pS[par][:],
                                                    tri[:], ALU.add),
                                    mark="dv")
                        src_ev = ev_m
                    b.waits("scalar", [src_ev, pe_z.get(g - 2)])
                    ev_e = b.do("scalar", lambda s, par=par: s.activation(
                        Pp[par][:], pS[par][:], AF.Exp, scale=SC_ATT),
                        mark="ac")
                    exp_evs[g] = ev_e
                    return ev_s

                def emit_av(kt, g, kvh=kvh, qt=qt):
                    par = g % 2
                    b.wait("tensor", exp_evs[g])
                    b.do("tensor", lambda t, kvh=kvh, kt=kt, qt=qt, par=par:
                         t.matmul(pAV[:],
                                  v_km[:, (kvh * 16 + kt) * 128:
                                       (kvh * 16 + kt + 1) * 128],
                                  Pp[par][:], start=(kt == 0),
                                  stop=(kt == qt)))
                    pe_z[g] = b.do(
                        "tensor", lambda t, kt=kt, qt=qt, par=par:
                        t.matmul(pZ[:], ones_bf[:], Pp[par][:],
                                 start=(kt == 0), stop=(kt == qt)),
                        mark="pe")
                    return pe_z[g]

                for kt in range(qt + 1):
                    last_smm = emit_smm(kt, g + kt)
                    if kt >= 1:
                        emit_av(kt - 1, g + kt - 1)
                ev_zlast = emit_av(qt, g + qt)
                g += qt + 1
                at = atile[ai % 2]
                b.waits("vector", [ev_zlast, at_free[ai % 2]])

                def evfl(v, at=at):
                    v.drain()
                    v.reciprocal(zrec[:], pZ[:])
                    v.drain()
                    return v.tensor_tensor(at[:], pAV[:], zrec[:], ALU.mult)
                ev_evac = b.do("vector", evfl, mark="dv")
                b.wait("gpsimd", ev_evac)
                at_free[ai % 2] = b.dma("gpsimd", at_sc[kvh, :, qt, :], at[:])
                ai += 1
            q_free[qt % 2] = last_smm
        b.barrier()
        p3.close()
        p3_ps.close()
    kv_ctx.close()

    if upto >= 6:
        p4 = ExitStack()
        e4 = p4.enter_context
        woutb = e4(nc.sbuf_tensor("woutb", [128, 32 * D], BF16))
        wst4 = [e4(nc.sbuf_tensor(f"wst4{j}", [128, D], F32))
                for j in range(2)]
        abuf = [e4(nc.sbuf_tensor(f"abuf{j}", [128, 4096], BF16))
                for j in range(2)]
        sst = [e4(nc.sbuf_tensor(f"sst{j}", [128, D], BF16))
               for j in range(2)]
        p4_ps = ExitStack()
        pO = [p4_ps.enter_context(nc.psum_tensor(f"pO{j}", [128, 512], F32))
              for j in range(2)]

        stg_free = [None, None]
        for h in range(32):
            sb = wst4[h % 2]
            b.wait("sync", stg_free[h % 2])
            ev_w = b.dma("sync", sb[:], wout_h[h * 128:(h + 1) * 128, :])
            b.wait("scalar", ev_w)
            ev_c = b.do("scalar", lambda s, sb=sb, h=h: s.copy(
                woutb[:, h * D:(h + 1) * D], sb[:]), mark="ac")
            stg_free[h % 2] = ev_c
        cast_done = stg_free[31 % 2]
        a_free = [None, None]
        s_free = [None, None]
        ev_ev4 = [None, None]
        for qt in range(NT):
            ab = abuf[qt % 2]
            b.wait("sync", a_free[qt % 2])
            ev_a = b.dma("sync", ab[:].rearrange("p (kvh f) -> p kvh f",
                                                 kvh=NKV),
                         at_sc[:, :, qt, :].rearrange("kvh hd f -> hd kvh f"))
            ss = sst[qt % 2]
            b.wait("vector", s_free[qt % 2])
            last_mm = None
            for dpb in range(4):
                ps = pO[dpb % 2]
                b.waits("tensor", [ev_a, cast_done, ev_ev4[dpb % 2]])

                def mmo(t, dpb=dpb, ps=ps, ab=ab):
                    ins = None
                    for h in range(32):
                        ins = t.matmul(ps[:],
                                       ab[:, (h // 4) * 512 + (h % 4) * 128:
                                          (h // 4) * 512 + (h % 4 + 1) * 128],
                                       woutb[:, h * D + dpb * 512:
                                             h * D + (dpb + 1) * 512],
                                       start=(h == 0), stop=(h == 31))
                    return ins
                ev_mm = b.do("tensor", mmo, mark="pe")
                last_mm = ev_mm
                b.wait("vector", ev_mm)
                ev_e = b.do("vector", lambda v, ps=ps, dpb=dpb, ss=ss:
                            v.tensor_copy(ss[:, dpb * 512:(dpb + 1) * 512],
                                          ps[:]), mark="dv")
                ev_ev4[dpb % 2] = ev_e
            a_free[qt % 2] = last_mm
            b.wait("gpsimd", ev_ev4[3 % 2])
            b.wait("gpsimd", ev_ev4[0])
            s_free[qt % 2] = b.dma("gpsimd", s_sc[qt], ss[:])
        b.barrier()
        p4.close()
        p4_ps.close()

    if upto >= 7:
        pf = ExitStack()
        ef = pf.enter_context
        xq1b = ef(nc.sbuf_tensor("xq1b", [128, ND * T], BF16))
        w1b = [ef(nc.sbuf_tensor(f"w1b{j}", [128, DI], BF16))
               for j in range(2)]
        b1bc = ef(nc.sbuf_tensor("b1bc", [128, DI], F32))
        b1row_t = ef(nc.sbuf_tensor("b1row_t", [1, DI], F32))
        b1row = b1row_t[0:1, :]
        ev_b1l = b.dma("sync", b1row, b1_h[0:1, :])
        htmp = [ef(nc.sbuf_tensor(f"htmp{j}", [128, 512], F32))
                for j in range(2)]
        hsil = [ef(nc.sbuf_tensor(f"hsil{j}", [128, 512], F32))
                for j in range(2)]
        pf_ps = ExitStack()
        pF = [pf_ps.enter_context(nc.psum_tensor(f"pF{j}", [128, 512], F32))
              for j in range(2)]
        pB5 = pf_ps.enter_context(nc.psum_tensor("pB5", [128, 512], F32))

        ev_xq = []
        for qt in range(NT):
            dst = xq1b[:].rearrange("p (dc t) -> p dc t", dc=ND)[
                :, :, qt * 128:(qt + 1) * 128]
            ev_xq.append(b.dma("sync", dst,
                               xq1_sc[qt].rearrange("dc d t -> d dc t")))
        b.wait("tensor", ev_b1l)
        for ob in range(NOB):
            ev_b = b.do("tensor", lambda t, ob=ob: t.matmul(
                pB5[:], onec[0:1, :], b1row[0:1, ob * 512:(ob + 1) * 512],
                start=True, stop=True), mark="pe")
            b.wait("vector", ev_b)
            ev_v = b.do("vector", lambda v, ob=ob: (
                v.tensor_copy(b1bc[:, ob * 512:(ob + 1) * 512], pB5[:]),
                v.drain())[0], mark="dv")
            b.wait("tensor", ev_v)

        w_free = [None, None]
        ui = 0
        h_free = [None, None]
        ev_evF = [None, None]
        for ob in range(NOB):
            wb_ = w1b[ob % 2]
            b.wait("sync", w_free[ob % 2])
            ev_w = b.dma("sync", wb_[:].rearrange("p (dc o) -> p dc o",
                                                  dc=ND),
                         w1t_sc[ob].rearrange("dc d o -> d dc o"))
            last_mm = None
            for qt in range(NT):
                ps = pF[ui % 2]
                b.waits("tensor", ev_xq + [ev_w, ev_evF[ui % 2]])

                def mmf(t, ob=ob, qt=qt, ps=ps, wb_=wb_):
                    ins = None
                    for dc in range(ND):
                        ins = t.matmul(ps[:],
                                       xq1b[:, dc * T + qt * 128:
                                            dc * T + (qt + 1) * 128],
                                       wb_[:, dc * 512:(dc + 1) * 512],
                                       start=(dc == 0), stop=(dc == ND - 1))
                    return ins
                ev_mm = b.do("tensor", mmf, mark="pe")
                last_mm = ev_mm
                b.waits("vector", [ev_mm, h_free[ui % 2]])

                def evf(v, ps=ps, qt=qt, ob=ob, j=ui % 2):
                    return v.scalar_tensor_tensor(
                        htmp[j][:], ps[:], c1c[:, qt:qt + 1],
                        b1bc[:, ob * 512:(ob + 1) * 512],
                        ALU.mult, ALU.add)
                ev_e = b.do("vector", evf, mark="dv")
                ev_evF[ui % 2] = ev_e
                b.wait("scalar", ev_e)
                ev_s = b.do("scalar", lambda s, j=ui % 2: s.activation(
                    hsil[j][:], htmp[j][:], AF.Silu), mark="ac")
                b.wait("gpsimd", ev_s)
                h_free[ui % 2] = b.dma(
                    "gpsimd", h_sc[qt, :, ob * 512:(ob + 1) * 512],
                    hsil[ui % 2][:])
                ui += 1
            w_free[ob % 2] = last_mm
        b.barrier()
        pf.close()
        pf_ps.close()

    if upto >= 8:
        pg = ExitStack()
        eg = pg.enter_context
        gbc = eg(nc.sbuf_tensor("gbc", [128, DI], F32))
        bbc = eg(nc.sbuf_tensor("bbc", [128, DI], F32))
        stq = eg(nc.sbuf_tensor("stq", [128, 16], F32))
        pg_ps = ExitStack()
        pG = pg_ps.enter_context(nc.psum_tensor("pG", [128, 512], F32))
        pT3 = pg_ps.enter_context(nc.psum_tensor("pT3", [128, 512], BF16))

        rows_ctx = ExitStack()
        growb = rows_ctx.enter_context(
            nc.sbuf_tensor("growb", [1, 2 * DI], F32))
        ev_gl = b.dma("sync", growb[0:1, 0:DI], g2_h[0:1, :])
        ev_bl = b.dma("sync", growb[0:1, DI:2 * DI], bb2_h[0:1, :])
        b.waits("tensor", [ev_gl, ev_bl])
        ev_gb_done = None
        for ri, dstt in ((0, gbc), (1, bbc)):
            for ob in range(NOB):
                ev_b = b.do("tensor", lambda t, ob=ob, ri=ri:
                            t.matmul(pG[:], onec[0:1, :],
                                     growb[0:1, ri * DI + ob * 512:
                                           ri * DI + (ob + 1) * 512],
                                     start=True, stop=True), mark="pe")
                b.wait("vector", ev_b)
                ev_v = b.do("vector", lambda v, ob=ob, dstt=dstt:
                            v.tensor_copy(dstt[:, ob * 512:(ob + 1) * 512],
                                          pG[:]), mark="dv")
                b.wait("tensor", ev_v)
                ev_gb_done = ev_v
        rows_ctx.close()
        hb = [eg(nc.sbuf_tensor(f"hb{j}", [128, DI], F32)) for j in range(2)]
        ab2 = eg(nc.sbuf_tensor("ab2", [128, DI], F32))
        xqt2 = eg(nc.sbuf_tensor("xqt2", [128, DI], BF16))
        x2st = eg(nc.sbuf_tensor("x2st", [128, DI], BF16))
        trashP = xqt2

        (ASUM, ASQ, MU2c, EX2c, VARc, VEPSc, LNc, INVc, SQ2, EXRc, LNRc,
         RST2, T2c, M2c, RM2c, QS2c) = range(16)
        h_free2 = [ev_gb_done, ev_gb_done]
        st2_free = None
        for qt in range(NT):
            hbb = hb[qt % 2]
            b.wait("sync", h_free2[qt % 2])
            ev_h = b.dma("sync", hbb[:], h_sc[qt])
            ev_ms = b.do("vector", lambda v: (
                v.memset(stq[:, ASUM:ASUM + 1], 0.0),
                v.memset(stq[:, ASQ:ASQ + 1], 0.0),
                v.memset(stq[:, SQ2:SQ2 + 1], 0.0))[-1], mark="dv")
            b.waits("scalar", [ev_h, ev_ms])

            def gelu_sq(s, hbb=hbb):
                s.activation(ab2[:], hbb[:], AF.Gelu,
                             accum_out=stq[:, ASUM:ASUM + 1])
                s.drain()
                return s.activation(trashP[:], ab2[:], AF.Square,
                                    accum_out=stq[:, ASQ:ASQ + 1])
            ev_g = b.do("scalar", gelu_sq, mark="ac")
            b.wait("vector", ev_g)

            def stats2(v):
                v.tensor_scalar_mul(stq[:, MU2c:MU2c + 1],
                                    stq[:, ASUM:ASUM + 1], 1.0 / DI)
                v.tensor_scalar_mul(stq[:, EX2c:EX2c + 1],
                                    stq[:, ASQ:ASQ + 1], 1.0 / DI)
                v.drain()
                v.tensor_mul(stq[:, VARc:VARc + 1], stq[:, MU2c:MU2c + 1],
                             stq[:, MU2c:MU2c + 1])
                v.drain()
                v.tensor_sub(stq[:, VARc:VARc + 1], stq[:, EX2c:EX2c + 1],
                             stq[:, VARc:VARc + 1])
                v.drain()
                return v.tensor_scalar_add(stq[:, VEPSc:VEPSc + 1],
                                           stq[:, VARc:VARc + 1], EPS)
            ev_s2 = b.do("vector", stats2, mark="dv")
            b.wait("scalar", ev_s2)
            ev_i2 = b.do("scalar", lambda s: (
                s.activation(stq[:, LNc:LNc + 1], stq[:, VEPSc:VEPSc + 1],
                             AF.Ln),
                s.drain(),
                s.activation(stq[:, INVc:INVc + 1], stq[:, LNc:LNc + 1],
                             AF.Exp, scale=NEG_HALF))[-1], mark="ac")
            b.wait("vector", ev_i2)

            def ln2(v):
                v.drain()
                v.tensor_scalar(ab2[:], ab2[:], stq[:, MU2c:MU2c + 1],
                                stq[:, INVc:INVc + 1], ALU.subtract,
                                ALU.mult)
                v.drain()
                v.tensor_mul(ab2[:], ab2[:], gbc[:])
                v.drain()
                v.tensor_add(ab2[:], ab2[:], bbc[:])
                v.drain()
                return v.reduce_max(stq[:, T2c:T2c + 1], ab2[:], AX.X,
                                    apply_absolute_value=True)
            ev_l2 = b.do("vector", ln2, mark="dv")
            b.wait("scalar", ev_l2)
            ev_sq2 = b.do("scalar", lambda s: s.activation(
                trashP[:], ab2[:], AF.Square,
                accum_out=stq[:, SQ2:SQ2 + 1]), mark="ac")
            b.wait("vector", ev_sq2)

            def preR(v):
                v.tensor_scalar_mul(stq[:, EXRc:EXRc + 1],
                                    stq[:, SQ2:SQ2 + 1], 1.0 / DI)
                v.drain()
                return v.tensor_scalar_add(stq[:, EXRc:EXRc + 1],
                                           stq[:, EXRc:EXRc + 1], EPS)
            ev_pr = b.do("vector", preR, mark="dv")
            b.wait("scalar", ev_pr)
            ev_r2 = b.do("scalar", lambda s: (
                s.activation(stq[:, LNRc:LNRc + 1], stq[:, EXRc:EXRc + 1],
                             AF.Ln),
                s.drain(),
                s.activation(stq[:, RST2:RST2 + 1], stq[:, LNRc:LNRc + 1],
                             AF.Exp, scale=NEG_HALF))[-1], mark="ac")
            b.wait("vector", ev_r2)

            def quant2(v, qt=qt, hbb=hbb):
                v.tensor_mul(stq[:, M2c:M2c + 1], stq[:, RST2:RST2 + 1],
                             stq[:, T2c:T2c + 1])
                v.drain()
                v.tensor_scalar_max(stq[:, M2c:M2c + 1],
                                    stq[:, M2c:M2c + 1], EPS)
                v.drain()
                v.tensor_scalar(c2c[:, qt:qt + 1], stq[:, M2c:M2c + 1],
                                wsc[:, 5:6], 1.0 / 127.0, ALU.mult, ALU.mult)
                v.reciprocal(stq[:, RM2c:RM2c + 1], stq[:, M2c:M2c + 1])
                v.drain()
                v.tensor_mul(stq[:, QS2c:QS2c + 1], stq[:, RST2:RST2 + 1],
                             stq[:, RM2c:RM2c + 1])
                v.drain()
                v.tensor_scalar_mul(stq[:, QS2c:QS2c + 1],
                                    stq[:, QS2c:QS2c + 1], 127.0)
                v.drain()
                v.tensor_scalar_mul(hbb[:], ab2[:], stq[:, QS2c:QS2c + 1])
                v.drain()
                return v.tensor_scalar(xqt2[:], hbb[:], MAGIC, MAGIC,
                                       ALU.add, ALU.subtract)
            ev_q2 = b.do("vector", quant2, mark="dv")
            h_free2[qt % 2] = ev_q2
            b.waits("tensor", [ev_q2, st2_free])
            last_ev = None
            for grp in range(NT):
                def trx(t, grp=grp):
                    ins = None
                    for j in range(4):
                        oc = grp * 4 + j
                        ins = t.transpose(pT3[:, j * 128:(j + 1) * 128],
                                          xqt2[:, oc * 128:(oc + 1) * 128],
                                          identb[:])
                    return ins
                ev_t = b.do("tensor", trx, mark="pe")
                b.wait("vector", ev_t)

                def evx(v, grp=grp):
                    dst = x2st[:].rearrange("p (oc t) -> p oc t", oc=NOC)[
                        :, grp * 4:(grp + 1) * 4, :]
                    return v.tensor_copy(
                        dst, pT3[:].rearrange("p (j t) -> p j t", j=4))
                last_ev = b.do("vector", evx, mark="dv")
                b.wait("tensor", last_ev)
            b.wait("gpsimd", last_ev)
            st2_free = b.dma(
                "gpsimd",
                xq2_sc[:, :, qt * 128:(qt + 1) * 128].rearrange(
                    "oc o t -> o oc t"),
                x2st[:].rearrange("p (oc t) -> p oc t", oc=NOC))
        b.barrier()
        pg.close()
        pg_ps.close()

    if upto >= 9:
        pm = ExitStack()
        em = pm.enter_context
        x2q = em(nc.sbuf_tensor("x2q", [128, NOC * 512], BF16))
        w2b = [em(nc.sbuf_tensor(f"w2b{j}", [128, NOC * 128], BF16))
               for j in range(2)]
        c2bc = em(nc.sbuf_tensor("c2bc", [128, 512], F32))
        c2r = em(nc.sbuf_tensor("c2r", [1, 128], F32))
        b2row_t = em(nc.sbuf_tensor("b2row_t", [1, D], F32))
        b2row = b2row_t[0:1, :]
        ev_b2l = b.dma("sync", b2row, b2_h[0:1, :])
        mtmp = em(nc.sbuf_tensor("mtmp", [128, 512], F32))
        gb_ = em(nc.sbuf_tensor("gb_", [128, 512], BF16))
        gq = em(nc.sbuf_tensor("gq", [128, 4 * D], BF16))
        xcb = em(nc.sbuf_tensor("xcb", [128, D], F32))
        scb2 = em(nc.sbuf_tensor("scb2", [128, D], BF16))
        sg32 = em(nc.sbuf_tensor("sg32", [128, D], F32))
        ocb = em(nc.sbuf_tensor("ocb", [128, D], F32))
        pm_ps = ExitStack()
        pM = [pm_ps.enter_context(nc.psum_tensor(f"pM{j}", [128, 512], F32))
              for j in range(2)]
        pTg = pm_ps.enter_context(nc.psum_tensor("pTg", [128, 512], BF16))
        pCt = pm_ps.enter_context(nc.psum_tensor("pCt", [1, 128], F32))
        pCb = pm_ps.enter_context(nc.psum_tensor("pCb", [128, 128], F32))

        b.wait("tensor", ev_b2l)
        for dpc in range(ND):
            ev_b = b.do("tensor", lambda t, dpc=dpc: t.matmul(
                pCb[:, 0:1], b2row[0:1, dpc * 128:(dpc + 1) * 128],
                one11[0:1, 0:1], start=True, stop=True), mark="pe")
            b.wait("vector", ev_b)
            ev_v = b.do("vector", lambda v, dpc=dpc: v.tensor_copy(
                b2cols[:, dpc:dpc + 1], pCb[:, 0:1]), mark="dv")
            b.wait("tensor", ev_v)

        x2_free = None
        w_free = [None, None]
        ev_evM = [None, None]
        ev_gev = None
        prev_sl = None
        prev_tg = None
        for quar in range(4):
            b.wait("sync", x2_free)
            ev_x2 = b.dma("sync",
                          x2q[:].rearrange("p (oc t) -> p oc t", oc=NOC),
                          xq2_sc[:, :, quar * 512:(quar + 1) * 512]
                          .rearrange("oc o t -> o oc t"))
            for j in range(4):
                qt = quar * 4 + j
                ev_t = b.do("tensor", lambda t, qt=qt: t.matmul(
                    pCt[0:1, :], c2c[:, qt:qt + 1], identf[:],
                    start=True, stop=True), mark="pe")
                b.wait("vector", ev_t)
                ev_v = b.do("vector", lambda v: v.tensor_copy(c2r[:],
                                                              pCt[0:1, :]),
                            mark="dv")
                b.wait("tensor", ev_v)
                ev_b2 = b.do("tensor", lambda t: t.matmul(
                    pCb[:], onec[0:1, :], c2r[0:1, :], start=True,
                    stop=True), mark="pe")
                b.wait("vector", ev_b2)
                ev_c = b.do("vector", lambda v, j=j: (
                    v.tensor_copy(c2bc[:, j * 128:(j + 1) * 128], pCb[:]),
                    v.drain())[0], mark="dv")
                b.wait("tensor", ev_c)

            for dpc in range(ND):
                wb2_ = w2b[dpc % 2]
                b.wait("sync", w_free[dpc % 2])
                ev_w = b.dma("sync",
                             wb2_[:].rearrange("p (oc dp) -> p oc dp",
                                               oc=NOC),
                             w2t_sc[dpc].rearrange("oc o dp -> o oc dp"))
                ps = pM[dpc % 2]
                b.waits("tensor", [ev_x2, ev_w, ev_evM[dpc % 2]])

                def mm2(t, ps=ps, wb2_=wb2_):
                    ins = None
                    for oc in range(NOC):
                        ins = t.matmul(ps[:],
                                       wb2_[:, oc * 128:(oc + 1) * 128],
                                       x2q[:, oc * 512:(oc + 1) * 512],
                                       start=(oc == 0), stop=(oc == NOC - 1))
                    return ins
                ev_mm = b.do("tensor", mm2, mark="pe")
                w_free[dpc % 2] = ev_mm
                if dpc == ND - 1:
                    x2_free = ev_mm
                b.waits("vector", [ev_mm, prev_sl])
                ev_e = b.do("vector", lambda v, ps=ps: v.tensor_tensor(
                    mtmp[:], ps[:], c2bc[:], ALU.mult), mark="dv")
                ev_evM[dpc % 2] = ev_e
                b.waits("scalar", [ev_e, prev_tg])
                ev_sl = b.do("scalar", lambda s, dpc=dpc: s.activation(
                    gb_[:], mtmp[:], AF.Silu,
                    bias=b2cols[:, dpc:dpc + 1]), mark="ac")
                prev_sl = ev_sl
                b.waits("tensor", [ev_sl, ev_gev])
                ev_tg = b.do("tensor", lambda t: (
                    [t.transpose(pTg[:, j * 128:(j + 1) * 128],
                                 gb_[:, j * 128:(j + 1) * 128], identb[:])
                     for j in range(4)])[-1], mark="pe")
                prev_tg = ev_tg
                b.wait("vector", ev_tg)

                def evg(v, dpc=dpc):
                    dst = gq[:].rearrange("p (tl d) -> p tl d", tl=4)[
                        :, :, dpc * 128:(dpc + 1) * 128]
                    return v.tensor_copy(
                        dst, pTg[:].rearrange("p (j t) -> p j t", j=4))
                ev_gev = b.do("vector", evg, mark="dv")
                b.wait("tensor", ev_gev)
            comb_free = None
            for tl in range(4):
                qt = quar * 4 + tl
                b.wait("sync", comb_free)
                ev_xr = b.dma("sync", xcb[:],
                              x_h[qt * 128:(qt + 1) * 128, :])
                ev_sr = b.dma("sync", scb2[:], s_sc[qt])
                b.waits("vector", [ev_xr, ev_sr, ev_gev])

                def comb(v, tl=tl):
                    v.drain()
                    v.tensor_mul(sg32[:], scb2[:],
                                 gq[:, tl * D:(tl + 1) * D])
                    v.drain()
                    return v.tensor_add(ocb[:], xcb[:], sg32[:])
                ev_c = b.do("vector", comb, mark="dv")
                comb_free = ev_c
                b.wait("gpsimd", ev_c)
                ev_y = b.dma("gpsimd", y_h[qt * 128:(qt + 1) * 128, :],
                             ocb[:])
                b.wait("vector", ev_y)
        pm.close()
        pm_ps.close()

    def fin(g):
        for s_ in ("dg0", "dg1", "dg2", "dg3"):
            g.wait_ge(b.semh[s_], b.cnt.get(s_, 0))
    b.emit("gpsimd", fin)

    with nc.Block() as block:
        @block.sync
        def _(eng):
            for f in b.ops["sync"]:
                f(eng)

        @block.tensor
        def _(eng):
            for f in b.ops["tensor"]:
                f(eng)

        @block.vector
        def _(eng):
            for f in b.ops["vector"]:
                f(eng)

        @block.scalar
        def _(eng):
            for f in b.ops["scalar"]:
                f(eng)

        @block.gpsimd
        def _(eng):
            for f in b.ops["gpsimd"]:
                f(eng)

    ctx.close()
    return nc


def _rope_tables():
    e_ = np.arange(0, 128, 2, dtype=np.float32) / np.float32(128.0)
    p_ = np.power(np.float32(10000.0), e_).astype(np.float32)
    invf = (np.float32(1.0) / p_).astype(np.float32)
    pos = np.arange(T, dtype=np.float32)
    freqs = (pos[:, None] * invf[None, :]).astype(np.float32)
    emb = np.concatenate([freqs, freqs], axis=1)
    cos = np.cos(emb).astype(np.float32).T.copy()
    sin = np.sin(emb).astype(np.float32).T.copy()
    ss = sin.copy()
    ss[0:64] = -sin[0:64]
    return np.ascontiguousarray(cos), np.ascontiguousarray(ss)


def kernel(L0_attn_u, L0_attn_W_in, L0_attn_W_out, L0_ffn_w1, L0_ffn_b1,
           L0_ffn_ln_g, L0_ffn_ln_b, L0_ffn_w2, L0_ffn_b2):
    nb = L0_attn_u.shape[0]
    wkvq = np.ascontiguousarray(np.asarray(L0_attn_W_in).T[:, 4096:10240],
                                dtype=np.float32)
    woutT = np.ascontiguousarray(np.asarray(L0_attn_W_out).T,
                                 dtype=np.float32)
    w1T = np.ascontiguousarray(np.asarray(L0_ffn_w1).T, dtype=np.float32)
    w2T = np.ascontiguousarray(np.asarray(L0_ffn_w2).T, dtype=np.float32)
    b1 = np.asarray(L0_ffn_b1, dtype=np.float32).reshape(1, DI)
    lng = np.asarray(L0_ffn_ln_g, dtype=np.float32).reshape(1, DI)
    lnb = np.asarray(L0_ffn_ln_b, dtype=np.float32).reshape(1, DI)
    b2 = np.asarray(L0_ffn_b2, dtype=np.float32).reshape(1, D)
    cosT, sinT = _rope_tables()
    base = np.where(np.arange(128)[:, None] > np.arange(128)[None, :],
                    np.float32(-1e30), np.float32(0.0)).astype(np.float32)
    trimask = np.ascontiguousarray(np.tile(base, (1, 4)))
    ident = np.eye(128, dtype=np.float32)

    nc = build_nc(debug=DEBUG, upto=UPTO)
    in_maps = []
    for i in range(nb):
        in_maps.append(dict(
            x=np.ascontiguousarray(np.asarray(L0_attn_u[i],
                                              dtype=np.float32)),
            wkvq=wkvq, woutT=woutT, w1T=w1T, w2T=w2T, b1=b1, lng=lng,
            lnb=lnb, b2=b2, cosT=cosT, sinT=sinT, trimask=trimask,
            ident=ident))
    res = run_bass_kernel_spmd(nc, in_maps, core_ids=list(range(nb)))
    out = np.stack([np.asarray(r["y"], dtype=np.float32)
                    for r in res.results], axis=0)
    if DEBUG:
        kernel.last_results = res
    return out
